# version 06
# baseline (speedup 1.0000x reference)
import numpy as np
from contextlib import ExitStack

import concourse.bass as bass
import concourse.mybir as mybir
from concourse.bass_utils import run_bass_kernel_spmd

F32 = mybir.dt.float32
BF16 = mybir.dt.bfloat16
U32 = mybir.dt.uint32
AF = mybir.ActivationFunctionType
ALU = mybir.AluOpType
AX = mybir.AxisListType

P = 128
NCORE = 8
T_CORE = 4096
D = 1024
E, HQ, FFH = 8, 4, 4096
TG = 512
NG = T_CORE // TG
NT = TG // P
LNEPS = 1e-5
EPS = 1e-5

_DEV = False


def AP(t, off, dims):
    return bass.AP(t, off, dims)


def emit_ffn_pass(nc, ctx, *, tag, src, dst, u_dram, w1, w3, w2, wg_sb,
                  sel_sb, idn_f, idn_b, dbg=None, semctx=None):
    semctx = semctx or ctx
    sems = {}
    for name in ["sX", "sW", "sH1", "sH3", "sSIL", "sHS", "sHID", "sW2",
                 "sW2f", "sTR", "sTRf", "sU1", "sR1", "sRS", "sG", "sGe",
                 "sD", "sSIG", "sCV", "sCVT", "sCVTe", "sCB", "sCBe",
                 "sOAT", "sRES", "sU2", "sSTX1", "sSTU", "sDBG", "sVC"]:
        sems[name] = semctx.enter_context(nc.semaphore(f"{tag}_{name}"))
    S = type("S", (), sems)

    def sb(name, shape, dtype=F32):
        return ctx.enter_context(nc.sbuf_tensor(f"{tag}_{name}", list(shape), dtype))

    xg = sb("xg", [P, 4 * D])
    u1tm = sb("u1tm", [P, 4 * D], BF16)
    u1fm = sb("u1fm", [P, 8 * TG], BF16)
    x1tm = sb("x1tm", [P, 4 * D])
    utm = sb("utm", [P, 4 * D], BF16)
    w1b = [sb(f"w1b{j}", [P, 8 * 1024], BF16) for j in range(2)]
    w3b = [sb(f"w3b{j}", [P, 8 * 1024], BF16) for j in range(2)]
    w2b = [sb(f"w2b{j}", [P, 8 * 1024], BF16) for j in range(2)]
    hidb = sb("hidb", [P, 8 * TG], BF16)
    silb = [sb(f"silb{j}", [P, TG]) for j in range(2)]
    h3s = sb("h3s", [P, TG])
    cvb_sb = sb("cvb", [P, TG])
    oacc = sb("oacc", [P, 8 * TG])
    lgb = sb("lgb", [P, 4 * 8])
    e1b = sb("e1b", [P, 4 * 8])
    e2b = sb("e2b", [P, 4 * 8])
    mkb = sb("mkb", [P, 4 * 8])
    cvt = sb("cvt", [P, 4 * 8])
    cvTb = sb("cvTb", [8, TG], BF16)
    bnb = sb("bnb", [P, 4 * 12])
    mvb = sb("mvb", [P, 4 * 2])
    ucol = sb("ucol", [P, 4])
    rcol = sb("rcol", [P, 4])
    rstdb = sb("rstdb", [P, 4])
    m1c = sb("m1c", [P, 4])
    m2c = sb("m2c", [P, 4])
    dcol = sb("dcol", [P, 4])
    w1c = sb("w1c", [P, 4])
    w2c = sb("w2c", [P, 4])
    ph1 = [ctx.enter_context(nc.psum_tensor(f"{tag}_ph1{j}", [P, TG], F32)) for j in range(2)]
    ph3 = [ctx.enter_context(nc.psum_tensor(f"{tag}_ph3{j}", [P, TG], F32)) for j in range(2)]
    pW = ctx.enter_context(nc.psum_tensor(f"{tag}_pW", [P, 1024], F32))
    pT = ctx.enter_context(nc.psum_tensor(f"{tag}_pT", [P, 1024], BF16))
    pA = ctx.enter_context(nc.psum_tensor(f"{tag}_pA", [P, TG], F32))
    pG = pA

    def wi_of(g, e, q):
        return g * 32 + e * 4 + q

    with nc.Block() as block:
        @block.sync
        def _(sync):
            for g in range(NG):
                if g > 0:
                    sync.wait_ge(S.sRES, g * 4)
                sync.dma_start(
                    xg[:, :],
                    AP(src, g * TG * D, [[D, P], [P * D, 4], [1, D]]),
                ).then_inc(S.sX, 16)
                sync.wait_ge(S.sRES, g * 4 + 4)
                sync.dma_start(
                    AP(dst, g * TG * D, [[D, P], [P * D, 4], [1, D]]),
                    x1tm[:, :],
                ).then_inc(S.sSTX1, 16)
                if u_dram is not None:
                    sync.wait_ge(S.sU2, g + 1)
                    sync.dma_start(
                        AP(u_dram, (1024 + g * TG) * D,
                           [[D, P], [P * D, 4], [1, D]]),
                        utm[:, :],
                    ).then_inc(S.sSTU, 16)
                if dbg is not None:
                    sync.wait_ge(S.sU1, g * 4 + 4)
                    sync.dma_start(
                        AP(dbg["u1"], g * TG * D, [[D, P], [P * D, 4], [1, D]]),
                        u1tm[:, :]).then_inc(S.sDBG, 16)
                    sync.wait_ge(S.sCV, g * 4 + 4)
                    sync.dma_start(
                        AP(dbg["cvt"], g * P * 32, [[32, P], [1, 32]]),
                        cvt[:, :]).then_inc(S.sDBG, 16)
                    sync.wait_ge(S.sW2f, (g + 1) * 128)
                    sync.dma_start(
                        AP(dbg["oa"], g * P * 4096, [[4096, P], [1, 4096]]),
                        oacc[:, :]).then_inc(S.sDBG, 16)
            if dbg is not None:
                sync.wait_ge(S.sDBG, NG * 3 * 16)
            sync.wait_ge(S.sSTX1, 16 * NG)
            if u_dram is not None:
                sync.wait_ge(S.sSTU, 16 * NG)

        @block.gpsimd
        def _(g_):
            for wi in range(NG * 32):
                g, r = divmod(wi, 32)
                e, q = divmod(r, 4)
                if wi >= 2:
                    g_.wait_ge(S.sW2f, (wi - 1) * 4)
                j = wi % 2
                g_.dma_start(
                    w1b[j][:, :],
                    AP(w1, e * D * FFH + q * 1024,
                       [[FFH, P], [P * FFH, 8], [1, 1024]]),
                ).then_inc(S.sW, 16)
                g_.dma_start(
                    w3b[j][:, :],
                    AP(w3, e * D * FFH + q * 1024,
                       [[FFH, P], [P * FFH, 8], [1, 1024]]),
                ).then_inc(S.sW, 16)
                g_.dma_start(
                    w2b[j][:, :],
                    AP(w2, e * FFH * D + q * 1024 * D,
                       [[D, P], [P * D, 8], [1, 1024]]),
                ).then_inc(S.sW, 16)

        @block.tensor
        def _(t):
            for g in range(NG):
                for ti in range(4):
                    tig = g * 4 + ti
                    t.wait_ge(S.sU1, tig + 1)
                    if tig > 0:
                        t.wait_ge(S.sTRf, tig)
                    for cc in range(8):
                        ins = t.transpose(
                            pT[:, cc * P:(cc + 1) * P],
                            u1tm[:, ti * D + cc * P: ti * D + (cc + 1) * P],
                            idn_b[:, :])
                    ins.then_inc(S.sTR, 1)
                t.wait_ge(S.sCV, g * 4)
                t.wait_ge(S.sCBe, g * 8)
                for ti in range(4):
                    t.wait_ge(S.sTRf, g * 4 + ti + 1)
                    for cc in range(8):
                        ins = t.matmul(pG[:, ti * 8:(ti + 1) * 8],
                                       u1fm[:, cc * TG + ti * P: cc * TG + (ti + 1) * P],
                                       wg_sb[:, cc * 8:(cc + 1) * 8],
                                       start=(cc == 0), stop=(cc == 7))
                    ins.then_inc(S.sG, 1)
                for ti in range(4):
                    t.wait_ge(S.sCV, g * 4 + ti + 1)
                    t.transpose(pA[0:8, ti * P:(ti + 1) * P],
                                cvt[:, ti * 8:(ti + 1) * 8],
                                idn_f[:, :]).then_inc(S.sCVT, 1)
                for e in range(E):
                    t.wait_ge(S.sCVTe, g + 1)
                    if e > 0:
                        t.wait_ge(S.sCBe, g * 8 + e)
                    t.matmul(pA[:, :], sel_sb[0:8, e * P:(e + 1) * P],
                             cvTb[0:8, :], start=True, stop=True).then_inc(S.sCB, 1)
                    for q in range(HQ):
                        wi = wi_of(g, e, q)
                        j = wi % 2
                        t.wait_ge(S.sW, 48 * (wi + 1))
                        if wi == 0:
                            t.wait_ge(S.sTRf, (g + 1) * 4)
                        for ho in range(8):
                            jho = wi * 8 + ho
                            if jho >= 2:
                                t.wait_ge(S.sSIL, jho - 1)
                            for cc in range(8):
                                ins = t.matmul(
                                    ph1[jho % 2][:, :],
                                    w1b[j][:, cc * 1024 + ho * P: cc * 1024 + (ho + 1) * P],
                                    u1fm[:, cc * TG:(cc + 1) * TG],
                                    start=(cc == 0), stop=(cc == 7))
                            ins.then_inc(S.sH1, 1)
                            if jho >= 2:
                                t.wait_ge(S.sHS, jho - 1)
                            for cc in range(8):
                                ins = t.matmul(
                                    ph3[jho % 2][:, :],
                                    w3b[j][:, cc * 1024 + ho * P: cc * 1024 + (ho + 1) * P],
                                    u1fm[:, cc * TG:(cc + 1) * TG],
                                    start=(cc == 0), stop=(cc == 7))
                            ins.then_inc(S.sH3, 1)
                        t.wait_ge(S.sHID, (wi + 1) * 8)
                        if wi == 0 and g == 0:
                            pass
                        for s in range(4):
                            if wi * 4 + s >= 1:
                                t.wait_ge(S.sW2f, wi * 4 + s)
                            if s == 0 and e == 0 and q == 0 and g > 0:
                                t.wait_ge(S.sRES, g * 4)
                            for jj in range(4):
                                idx = s * 4 + jj
                                dd, th = idx // 2, idx % 2
                                for hc in range(8):
                                    ins = t.matmul(
                                        pW[:, jj * 256:(jj + 1) * 256],
                                        w2b[j][:, hc * 1024 + dd * P: hc * 1024 + (dd + 1) * P],
                                        hidb[:, hc * TG + th * 256: hc * TG + (th + 1) * 256],
                                        start=(hc == 0), stop=(hc == 7))
                            ins.then_inc(S.sW2, 1)
                t.wait_ge(S.sW2f, (g + 1) * 128)
                for ti in range(4):
                    if ti > 0:
                        t.wait_ge(S.sRES, g * 4 + ti)
                    for dd in range(8):
                        ins = t.transpose(
                            pW[:, dd * P:(dd + 1) * P],
                            oacc[:, dd * TG + ti * P: dd * TG + (ti + 1) * P],
                            idn_f[:, :])
                    ins.then_inc(S.sOAT, 1)

        @block.vector
        def _(v):
            vcnt = [0]
            def vsync(v_, ins):
                vcnt[0] += 1
                ins.then_inc(S.sVC, 1)
                v_.wait_ge(S.sVC, vcnt[0])
            for g in range(NG):
                v.wait_ge(S.sX, 16 * (g + 1))
                if g > 0 and u_dram is not None:
                    v.wait_ge(S.sSTU, 16 * g)
                v.wait_ge(S.sSTX1, 16 * g)
                for ti in range(4):
                    v.bn_stats(bnb[:, ti * 12: ti * 12 + 6],
                               xg[:, ti * D: ti * D + 512])
                    v.bn_stats(bnb[:, ti * 12 + 6: ti * 12 + 12],
                               xg[:, ti * D + 512: ti * D + 1024])
                for ti in range(4):
                    ins = v.bn_aggr(mvb[:, ti * 2: ti * 2 + 2],
                                    bnb[:, ti * 12: ti * 12 + 12])
                vsync(v, ins)
                ins = v.tensor_scalar(ucol[:, :], mvb[:, 1::2], LNEPS, None, ALU.add)
                vsync(v, ins)
                v.reciprocal(rcol[:, :], ucol[:, :]).then_inc(S.sR1, 1)
                v.wait_ge(S.sRS, (2 * g + 1) if u_dram is not None else (g + 1))
                for ti in range(4):
                    ins = v.tensor_scalar(
                        u1tm[:, ti * D:(ti + 1) * D],
                        xg[:, ti * D:(ti + 1) * D],
                        mvb[:, ti * 2: ti * 2 + 1],
                        rstdb[:, ti: ti + 1],
                        ALU.subtract, ALU.mult)
                    ins.then_inc(S.sU1, 1)
                for ti in range(4):
                    v.wait_ge(S.sTR, g * 4 + ti + 1)
                    v.tensor_copy(
                        AP(u1fm, ti * P, [[8 * TG, P], [TG, 8], [1, P]]),
                        AP(pT, 0, [[1024, P], [P, 8], [1, P]]),
                    ).then_inc(S.sTRf, 1)
                for ti in range(4):
                    v.wait_ge(S.sG, g * 4 + ti + 1)
                    ins = v.tensor_copy(lgb[:, ti * 8:(ti + 1) * 8],
                                        pG[:, ti * 8:(ti + 1) * 8])
                vsync(v, ins)
                for ti in range(4):
                    ins = v.reduce_max(m1c[:, ti: ti + 1], lgb[:, ti * 8:(ti + 1) * 8], axis=AX.X)
                vsync(v, ins)
                for ti in range(4):
                    ins = v.tensor_scalar(e1b[:, ti * 8:(ti + 1) * 8],
                                          lgb[:, ti * 8:(ti + 1) * 8],
                                          m1c[:, ti: ti + 1], None, ALU.is_ge)
                vsync(v, ins)
                for ti in range(4):
                    ins = v.scalar_tensor_tensor(mkb[:, ti * 8:(ti + 1) * 8],
                                                 e1b[:, ti * 8:(ti + 1) * 8], -1e30,
                                                 lgb[:, ti * 8:(ti + 1) * 8],
                                                 ALU.mult, ALU.add)
                vsync(v, ins)
                for ti in range(4):
                    ins = v.reduce_max(m2c[:, ti: ti + 1], mkb[:, ti * 8:(ti + 1) * 8], axis=AX.X)
                vsync(v, ins)
                v.tensor_tensor(dcol[:, :], m1c[:, :], m2c[:, :], ALU.subtract).then_inc(S.sD, 1)
                for ti in range(4):
                    v.tensor_scalar(e2b[:, ti * 8:(ti + 1) * 8],
                                    mkb[:, ti * 8:(ti + 1) * 8],
                                    m2c[:, ti: ti + 1], None, ALU.is_ge)
                v.wait_ge(S.sSIG, g + 1)
                ins = v.tensor_scalar(w2c[:, :], w1c[:, :], -1.0, 1.0, ALU.mult, ALU.add)
                for ti in range(4):
                    ins = v.tensor_scalar(cvt[:, ti * 8:(ti + 1) * 8],
                                          e1b[:, ti * 8:(ti + 1) * 8],
                                          w1c[:, ti: ti + 1], None, ALU.mult)
                vsync(v, ins)
                for ti in range(4):
                    ins = v.scalar_tensor_tensor(
                        cvt[:, ti * 8:(ti + 1) * 8],
                        e2b[:, ti * 8:(ti + 1) * 8],
                        w2c[:, ti: ti + 1],
                        cvt[:, ti * 8:(ti + 1) * 8],
                        ALU.mult, ALU.add)
                    ins.then_inc(S.sCV, 1)
                v.wait_ge(S.sCVT, g * 4 + 4)
                v.tensor_copy(cvTb[0:8, :], pA[0:8, :]).then_inc(S.sCVTe, 1)
                if g > 0:
                    v.wait_ge(S.sOAT, g * 4)
                v.memset(oacc[:, :], 0.0)
                for e in range(E):
                    v.wait_ge(S.sCB, g * 8 + e + 1)
                    v.tensor_copy(cvb_sb[:, :], pA[:, :]).then_inc(S.sCBe, 1)
                    for q in range(HQ):
                        wi = wi_of(g, e, q)
                        for ho in range(8):
                            jho = wi * 8 + ho
                            v.wait_ge(S.sH3, jho + 1)
                            v.tensor_tensor(h3s[:, :], ph3[jho % 2][:, :],
                                            cvb_sb[:, :], ALU.mult).then_inc(S.sHS, 1)
                            v.wait_ge(S.sSIL, jho + 1)
                            v.tensor_tensor(hidb[:, ho * TG:(ho + 1) * TG],
                                            h3s[:, :], silb[jho % 2][:, :],
                                            ALU.mult).then_inc(S.sHID, 1)
                        for s in range(4):
                            v.wait_ge(S.sW2, wi * 4 + s + 1)
                            for jj in range(4):
                                idx = s * 4 + jj
                                dd, th = idx // 2, idx % 2
                                ins = v.tensor_tensor(
                                    oacc[:, dd * TG + th * 256: dd * TG + (th + 1) * 256],
                                    oacc[:, dd * TG + th * 256: dd * TG + (th + 1) * 256],
                                    pW[:, jj * 256:(jj + 1) * 256], ALU.add)
                            ins.then_inc(S.sW2f, 1)
                for ti in range(4):
                    v.wait_ge(S.sOAT, g * 4 + ti + 1)
                    for dd in range(8):
                        ins = v.scalar_tensor_tensor(
                            x1tm[:, ti * D + dd * P: ti * D + (dd + 1) * P],
                            pW[:, dd * P:(dd + 1) * P], 0.5,
                            xg[:, ti * D + dd * P: ti * D + (dd + 1) * P],
                            ALU.mult, ALU.add)
                    ins.then_inc(S.sRES, 1)
                if u_dram is not None:
                    for ti in range(4):
                        v.bn_stats(bnb[:, ti * 12: ti * 12 + 6],
                                   x1tm[:, ti * D: ti * D + 512])
                        v.bn_stats(bnb[:, ti * 12 + 6: ti * 12 + 12],
                                   x1tm[:, ti * D + 512: ti * D + 1024])
                    for ti in range(4):
                        ins = v.bn_aggr(mvb[:, ti * 2: ti * 2 + 2],
                                        bnb[:, ti * 12: ti * 12 + 12])
                    vsync(v, ins)
                    ins = v.tensor_scalar(ucol[:, :], mvb[:, 1::2], LNEPS, None, ALU.add)
                    vsync(v, ins)
                    v.reciprocal(rcol[:, :], ucol[:, :]).then_inc(S.sR1, 1)
                    v.wait_ge(S.sRS, 2 * g + 2)
                    for ti in range(4):
                        ins = v.tensor_scalar(
                            utm[:, ti * D:(ti + 1) * D],
                            x1tm[:, ti * D:(ti + 1) * D],
                            mvb[:, ti * 2: ti * 2 + 1],
                            rstdb[:, ti: ti + 1],
                            ALU.subtract, ALU.mult)
                    ins.then_inc(S.sU2, 1)

        @block.scalar
        def _(a):
            for g in range(NG):
                a.wait_ge(S.sR1, (2 * g + 1) if u_dram is not None else (g + 1))
                a.activation(rstdb[:, :], rcol[:, :], AF.Sqrt).then_inc(S.sRS, 1)
                a.wait_ge(S.sD, g + 1)
                a.activation(w1c[:, :], dcol[:, :], AF.Sigmoid).then_inc(S.sSIG, 1)
                for e in range(E):
                    for q in range(HQ):
                        wi = wi_of(g, e, q)
                        for ho in range(8):
                            jho = wi * 8 + ho
                            a.wait_ge(S.sH1, jho + 1)
                            if jho >= 2:
                                a.wait_ge(S.sHID, jho - 1)
                            a.activation(silb[jho % 2][:, :], ph1[jho % 2][:, :],
                                         AF.Silu).then_inc(S.sSIL, 1)
                if u_dram is not None:
                    a.wait_ge(S.sR1, 2 * g + 2)
                    a.activation(rstdb[:, :], rcol[:, :], AF.Sqrt).then_inc(S.sRS, 1)


class Sched:
    ENG = ("t", "v", "a", "s", "g")

    def __init__(self, nc, ctx, tag):
        self.nc = nc
        self.sem = {e: ctx.enter_context(nc.semaphore(f"{tag}_{e}")) for e in self.ENG}
        self.cnt = {e: 0 for e in self.ENG}
        self.items = {e: [] for e in self.ENG}
        self.seq = 0

    def emit(self, e, fn, deps=(), inc=1, small=False, last_inc=None):
        deps = [d for d in deps if d is not None]
        self.cnt[e] += inc
        tok = (e, self.cnt[e], small)
        self.items[e].append((fn, deps, last_inc if last_inc is not None else inc,
                              self.seq))
        self.seq += 1
        return tok

    def run(self, block, limit=None):
        def mk(e):
            def body(eng):
                done = 0
                for fn, deps, inc, seq in self.items[e]:
                    if limit is not None and seq >= limit:
                        break
                    for d in deps:
                        de, dn = d[0], d[1]
                        dsmall = d[2] if len(d) > 2 else False
                        if de == e and not dsmall:
                            continue
                        eng.wait_ge(self.sem[de], dn)
                    ins = fn(eng)
                    if ins is not None:
                        ins.then_inc(self.sem[e], inc)
                    else:
                        raise RuntimeError("step fn returned no instruction")
                    done += inc
                eng.wait_ge(self.sem[e], done)
            return body
        block.tensor(mk("t"))
        block.vector(mk("v"))
        block.scalar(mk("a"))
        block.sync(mk("s"))
        block.gpsimd(mk("g"))


def emit_attn(nc, ctx, ctxs, semctx, C):
    TBW = 256
    NTB = 16
    sc = Sched(nc, semctx, "at")
    E = sc.emit

    def sb(name, shape, dtype=F32):
        return ctx.enter_context(nc.sbuf_tensor(f"a_{name}", list(shape), dtype))

    def sbs(name, shape, dtype=F32):
        return ctxs.enter_context(nc.sbuf_tensor(f"a_{name}", list(shape), dtype,
                                                 side="right"))

    idn_f, tri_b, ones_c = C["idn_f"], C["tri_b"], C["ones_c"]

    uring = sb("uring", [P, 11 * 1024], BF16)
    tbk = [sb(f"tbk{j}", [P, 1920], BF16) for j in range(2)]
    mphb = [sb(f"mphb{j}", [P, 4096], BF16) for j in range(2)]
    wkv = [sb(f"wkv{j}", [P, 4096], BF16) for j in range(2)]
    cv8 = sb("cv8", [P, 8 * TBW], BF16)
    xtb = sb("xtb", [P, 8 * TBW], BF16)
    q64b = sb("q64b", [64, 16 * TBW], BF16)
    kn64b = sb("kn64b", [64, 16 * TBW], BF16)
    vnb = sb("vnb", [64, 16 * TBW], BF16)
    raw = sb("raw", [64, 16 * TBW])
    prodb = sb("prodb", [64, 8 * TBW])
    gkb = sb("gkb", [64, 2 * TBW], BF16)
    nkc = sb("nkc", [16, TBW])
    nsc = sb("nsc", [16, TBW])
    rcpn = sb("rcpn", [16, TBW])
    glg = sb("glg", [16, TBW])
    gt1 = sb("gt1", [16, TBW])
    gt2 = sb("gt2", [16, TBW])
    halob = sb("halob", [P, 1024], BF16)
    hoff = sb("hoff", [1, 1], U32)
    hmask = sb("hmask", [P, 1])
    eh64f = sb("eh64f", [64, 16 * 16])
    eh64b = sb("eh64b", [64, 16 * 16], BF16)
    rh16 = sb("rh16", [16, 16 * 64])
    GT64 = sb("GT64", [64, 64], BF16)
    mphib = sb("mphib", [P, 8])
    qbc = sb("qbc", [64, 16])
    kbc = sb("kbc", [64, 16])
    vbc = sb("vbc", [64, 16])
    kvns = sb("kvns", [16, 1])
    wgzb = sb("wgzb", [16, 1])
    b8s = sb("b8s", [8, P])
    b16s = sb("b16s", [16, 64])
    Ssb = sbs("Ssb", [64, 1024])
    sim_sb = sbs("sim_sb", [16, 4096])
    g_sb = sbs("g_sb", [16, 4096])
    g2_sb = sbs("g2_sb", [16, 4096])
    sumq_sb = sbs("sumq_sb", [16, 4096])
    wobc = sbs("wobc", [P, 8])
    pc0 = ctxs.enter_context(nc.psum_tensor("a_pc0", [P, 512], F32))
    pc1 = ctxs.enter_context(nc.psum_tensor("a_pc1", [P, 512], F32))
    pxt = ctxs.enter_context(nc.psum_tensor("a_pxt", [P, 2048], F32))
    pqk = ctxs.enter_context(nc.psum_tensor("a_pqk", [64, 1024], F32))

    U_ext, tbank, mphiT = C["U_ext"], C["tbank"], C["mphiT"]
    QFM, KFM, VFM = C["QFM"], C["KFM"], C["VFM"]

    r = {"pc0": None, "pc1": None, "pxt": None, "pqk": None,
         "tbk0": None, "tbk1": None, "mph0": None, "mph1": None,
         "wkv0": None, "wkv1": None, "xtb": None}

    def build_consts(g):
        for h in range(16):
            for tile in (eh64f, eh64b):
                g.memset(tile[:, h * 16:(h + 1) * 16], 1.0)
                g.affine_select(tile[:, h * 16:(h + 1) * 16],
                                tile[:, h * 16:(h + 1) * 16],
                                [[1, 16]], ALU.is_equal, 0.0,
                                base=-h, channel_multiplier=0)
            g.memset(rh16[:, h * 64:(h + 1) * 64], 1.0)
            ins = g.affine_select(rh16[:, h * 64:(h + 1) * 64],
                                  rh16[:, h * 64:(h + 1) * 64],
                                  [[0, 64]], ALU.is_equal, 0.0,
                                  base=-h, channel_multiplier=1)
        return ins
    tko = E("g", build_consts)
    tk_z = E("g", lambda g: g.memset(Ssb[:, :], 0.0))
    tk_gt = E("g", lambda g: g.dma_start(GT64[:, :], C["gt"][:, :]), inc=16)
    tk_sc = E("s", lambda s: s.dma_start(kvns[:, :], C["kvns"][:, :]), inc=16)
    tk_sc2 = E("s", lambda s: s.dma_start(wgzb[:, :], C["wgzb"][:, :]), inc=16)
    tk_hm = E("s", lambda s: s.dma_start(hmask[:, :], C["halo_mask"][:, :]), inc=16)
    tk_ho = E("s", lambda s: s.dma_start(hoff[:, :], C["halo_off"][:, :]), inc=16)

    bias_specs = [(C["bias_dd"], 0, b8s, 8, mphib),
                  (C["bias_dd"], 8 * P, b8s, 8, wobc),
                  (C["bias_h"], 0, b16s, 16, qbc),
                  (C["bias_h"], 16 * 64, b16s, 16, kbc),
                  (C["bias_h"], 2 * 16 * 64, b16s, 16, vbc)]
    prev_ev = None
    for (srcd, off, stg, npart, dst) in bias_specs:
        w = stg.shape[1]
        ld = E("s", (lambda srcd, off, stg, npart, w: lambda s: s.dma_start(
            stg[:, :], AP(srcd, off, [[w, npart], [1, w]])))(srcd, off, stg, npart, w),
            deps=[prev_ev], inc=16)
        tr = E("t", (lambda stg, npart, w: lambda t: t.transpose(
            pc0[0:w, 0:npart], stg[:, :], idn_f[0:npart, 0:npart]))(stg, npart, w),
            deps=[ld, r["pc0"]])
        ev = E("v", (lambda dst, npart, w: lambda v: v.tensor_copy(
            dst[:, :], pc0[0:w, 0:npart]))(dst, npart, w), deps=[tr])
        r["pc0"] = ev
        prev_ev = ev

    ccu_in, ccu_out = C["ccu_in"], C["ccu_out"]
    cp = E("g", lambda g: g.dma_start(ccu_in[:, :],
                                      AP(U_ext, 4096 * 1024, [[1024, 1024], [1, 1024]])),
           deps=[C["u_done"]], inc=16)
    cc = E("g", lambda g: g.collective_compute(
        "AllGather", ALU.bypass, replica_groups=[[0, 1, 2, 3, 4, 5, 6, 7]],
        ins=[ccu_in.ap().opt()], outs=[ccu_out.ap().opt()]), deps=[cp], inc=1)
    hs = None
    for hh in range(8):
        def halo_load(g, hh=hh):
            with g.register(f"hreg{hh}") as hr:
                g.reg_load(hr, hoff[0:1, 0:1])
                g.reg_add(hr, hr, hh * P * 1024)
                ins = g.dma_start(
                    bass.AP(halob, 0, [[1024, P], [1, 1024]]),
                    bass.AP(ccu_out, hr, [[1024, P], [1, 1024]]))
            return ins
        hl = E("g", halo_load, deps=[cc, tk_ho, hs], inc=16)
        hm = E("v", lambda v: v.tensor_scalar(halob[:, :], halob[:, :], hmask[:, 0:1],
                                              None, ALU.mult), deps=[hl, tk_hm])
        hs = E("s", (lambda hh: lambda s: s.dma_start(
            AP(U_ext, hh * P * 1024, [[1024, P], [1, 1024]]),
            halob[:, :]))(hh), deps=[hm], inc=16)

    uc_tok = {}
    kd_tok = {}
    store_deps = []
    for tb in range(NTB):
        cis = list(range(0, 10)) if tb == 0 else [2 * tb + 8, 2 * tb + 9]
        for ci in cis:
            dep = kd_tok.get((tb - 6, 15))
            uc_tok[ci] = E("s", (lambda ci: lambda s: s.dma_start(
                uring[:, (ci % 11) * 1024:((ci % 11) + 1) * 1024],
                AP(U_ext, ci * P * 1024, [[1024, P], [1, 1024]])))(ci),
                deps=[dep, hs], inc=16)
        for k in range(16):
            idx = tb * 16 + k
            jj = idx % 2
            tb_tok = E("g", (lambda k, jj: lambda g: g.dma_start(
                tbk[jj][:, :], AP(tbank, k * P * 1920, [[1920, P], [1, 1920]])))(k, jj),
                deps=[r[f"tbk{jj}"]], inc=16)
            mp_tok = E("g", (lambda k, jj: lambda g: g.dma_start(
                mphb[jj][:, :], AP(mphiT, k * 1024 * 1024,
                                   [[1024, P], [P * 1024, 4], [1, 1024]])))(k, jj),
                deps=[r[f"mph{jj}"]], inc=16)
            mp_tok2 = E("g", (lambda k, jj: lambda g: g.dma_start(
                mphb[1 - jj][:, :], AP(mphiT, k * 1024 * 1024 + 4 * P * 1024,
                                       [[1024, P], [P * 1024, 4], [1, 1024]])))(k, jj),
                deps=[r[f"mph{1 - jj}"]], inc=16)
            ev_toks = []
            for c in range(8):
                sn = idx * 8 + c
                pcb = pc0 if sn % 2 == 0 else pc1
                pcn = "pc0" if sn % 2 == 0 else "pc1"

                def conv_fn(t, tb=tb, c=c, jj=jj, pcb=pcb):
                    for m in range(10):
                        ci = 2 * tb + m
                        ins = t.matmul(
                            pcb[:, 0:TBW],
                            uring[:, (ci % 11) * 1024 + c * P:
                                  (ci % 11) * 1024 + (c + 1) * P],
                            tbk[jj][:, 1408 - 128 * m: 1664 - 128 * m],
                            start=(m == 0), stop=(m == 9))
                    return ins
                udeps = [uc_tok[2 * tb + m] for m in range(10)] if c == 0 else []
                cvt_ = E("t", conv_fn, deps=[tb_tok, r[pcn]] + udeps)
                ev = E("v", (lambda c, pcb: lambda v: v.tensor_copy(
                    cv8[:, c * TBW:(c + 1) * TBW], pcb[:, 0:TBW]))(c, pcb),
                    deps=[cvt_, kd_tok.get((tb, k - 1)) if k > 0
                          else kd_tok.get((tb - 1, 15))])
                r[pcn] = ev
                ev_toks.append(ev)
            r[f"tbk{jj}"] = cvt_

            def mph_fn(t, k=k, jj=jj):
                for dd in range(8):
                    for c in range(8):
                        buf = mphb[jj] if c < 4 else mphb[1 - jj]
                        ins = t.matmul(
                            pxt[:, dd * TBW:(dd + 1) * TBW],
                            buf[:, (c % 4) * 1024 + dd * P: (c % 4) * 1024 + (dd + 1) * P],
                            cv8[:, c * TBW:(c + 1) * TBW],
                            start=(k == 0 and c == 0 and dd % 2 == 0),
                            stop=(k == 15 and c == 7))
                return ins
            kd_tok[(tb, k)] = E("t", mph_fn,
                                deps=ev_toks + [mp_tok, mp_tok2] +
                                ([r["pxt"]] if k == 0 else []))
            r[f"mph{jj}"] = kd_tok[(tb, k)]
            r[f"mph{1 - jj}"] = kd_tok[(tb, k)]

        def xt_fn(v):
            for dd in range(8):
                ins = v.tensor_scalar(xtb[:, dd * TBW:(dd + 1) * TBW],
                                      pxt[:, dd * TBW:(dd + 1) * TBW],
                                      mphib[:, dd: dd + 1], None, ALU.add)
            return ins
        xte = E("v", xt_fn, deps=[kd_tok[(tb, 15)], r["xtb"], prev_ev])
        r["pxt"] = xte

        qk_ev = {}
        wl = {}
        for ti in range(3):
            wname = ["wqT", "wkT", "wvT"][ti]
            for hf in range(2):
                wjj = (tb * 6 + ti * 2 + hf) % 2
                wl[(ti, hf)] = E("g", (lambda wname, hf, wjj: lambda g: g.dma_start(
                    wkv[wjj][:, :], AP(C[wname], hf * 4 * P * 1024,
                                       [[1024, P], [P * 1024, 4], [1, 1024]])))(wname, hf, wjj),
                    deps=[r[f"wkv{wjj}"]], inc=16)
            for qq in range(4):
                def qkv_fn(t, ti=ti, qq=qq, tb=tb):
                    for h4 in range(4):
                        h = qq * 4 + h4
                        for cc in range(8):
                            hf, c4 = cc // 4, cc % 4
                            wjj = (tb * 6 + ti * 2 + hf) % 2
                            ins = t.matmul(
                                pqk[:, h4 * TBW:(h4 + 1) * TBW],
                                wkv[wjj][:, c4 * 1024 + h * 64: c4 * 1024 + (h + 1) * 64],
                                xtb[:, cc * TBW:(cc + 1) * TBW],
                                start=(cc == 0 and h4 % 2 == 0), stop=(cc == 7))
                    return ins
                mmq = E("t", qkv_fn, deps=[xte, r["pqk"], wl[(ti, 0)], wl[(ti, 1)]])

                def qev_fn(v, ti=ti, qq=qq):
                    bias = (qbc, kbc, vbc)[ti]
                    out = (q64b, raw, raw)[ti]
                    for h4 in range(4):
                        h = qq * 4 + h4
                        ins = v.tensor_scalar(out[:, h * TBW:(h + 1) * TBW],
                                              pqk[:, h4 * TBW:(h4 + 1) * TBW],
                                              bias[:, h: h + 1], None, ALU.add)
                    return ins
                dep_raw = []
                if ti == 2 and qq == 0:
                    dep_raw = [qk_ev.get(("nrm", 0))]
                qk_ev[(ti, qq)] = E("v", qev_fn, deps=[mmq] + dep_raw)
                r["pqk"] = qk_ev[(ti, qq)]
            r["wkv0"] = mmq
            r["wkv1"] = mmq
            if ti >= 1:
                ni = ti - 1
                dsrc = [qk_ev[(ti, qq)] for qq in range(4)]
                nkm = None
                for half in range(2):
                    sqt = E("a", (lambda half: lambda a: a.activation(
                        prodb[:, :], raw[:, half * 8 * TBW:(half + 1) * 8 * TBW],
                        AF.Square))(half), deps=dsrc + [nkm])

                    def nk_fn(t, half=half):
                        for hh in range(8):
                            ins = t.matmul(pc0[0:16, 0:TBW],
                                           eh64f[:, (half * 8 + hh) * 16:
                                                 (half * 8 + hh + 1) * 16],
                                           prodb[:, hh * TBW:(hh + 1) * TBW],
                                           start=(half == 0 and hh == 0),
                                           stop=(half == 1 and hh == 7))
                        return ins
                    nkm = E("t", nk_fn, deps=[sqt, r["pc0"] if half == 0 else None, tko])
                nke = E("v", lambda v: v.tensor_copy(nkc[:, :], pc0[0:16, 0:TBW]),
                        deps=[nkm])
                r["pc0"] = nke
                nss = E("a", lambda a: a.activation(nsc[:, :], nkc[:, :], AF.Sqrt),
                        deps=[nke])
                mx = E("v", lambda v: v.tensor_scalar(nsc[:, :], nsc[:, :], 1e-12,
                                                      None, ALU.max), deps=[nss],
                       small=True)
                rc = E("v", lambda v: v.reciprocal(rcpn[:, :], nsc[:, :]),
                       deps=[mx], small=True)
                last_tt = None
                dstb = kn64b if ni == 0 else vnb
                for hp in range(8):
                    def bc_fn(t, hp=hp):
                        for j2 in range(2):
                            h = hp * 2 + j2
                            ins = t.matmul(pc1[0:64, j2 * TBW:(j2 + 1) * TBW],
                                           rh16[:, h * 64:(h + 1) * 64], rcpn[:, :],
                                           start=True, stop=True)
                        return ins
                    bcm = E("t", bc_fn, deps=[rc, r["pc1"]])

                    def tt_fn(v, hp=hp, dstb=dstb):
                        for j2 in range(2):
                            h = hp * 2 + j2
                            ins = v.tensor_tensor(dstb[:, h * TBW:(h + 1) * TBW],
                                                  raw[:, h * TBW:(h + 1) * TBW],
                                                  pc1[0:64, j2 * TBW:(j2 + 1) * TBW],
                                                  ALU.mult)
                        return ins
                    last_tt = E("v", tt_fn, deps=[bcm])
                    r["pc1"] = last_tt
                qk_ev[("nrm", ni)] = last_tt
        knb = qk_ev[("nrm", 0)]
        vnb_t = qk_ev[("nrm", 1)]
        qdone = [qk_ev[(0, qq)] for qq in range(4)]

        simm = None
        prod_s = None
        for half in range(2):
            prod_s = E("v", (lambda half, simm=None: lambda v: v.tensor_tensor(
                prodb[:, :], q64b[:, half * 8 * TBW:(half + 1) * 8 * TBW],
                kn64b[:, half * 8 * TBW:(half + 1) * 8 * TBW], ALU.mult))(half),
                deps=[knb, vnb_t, simm] + qdone)

            def sim_fn(t, half=half):
                for hh in range(8):
                    ins = t.matmul(pc0[0:16, 0:TBW],
                                   eh64f[:, (half * 8 + hh) * 16:
                                         (half * 8 + hh + 1) * 16],
                                   prodb[:, hh * TBW:(hh + 1) * TBW],
                                   start=(half == 0 and hh == 0),
                                   stop=(half == 1 and hh == 7))
                return ins
            simm = E("t", sim_fn, deps=[prod_s, r["pc0"] if half == 0 else None])
        sime = E("v", (lambda tb: lambda v: v.tensor_copy(
            sim_sb[:, tb * TBW:(tb + 1) * TBW], pc0[0:16, 0:TBW]))(tb), deps=[simm])
        r["pc0"] = sime

        def sumq_fn(t):
            for h in range(16):
                ins = t.matmul(pc0[0:16, 0:TBW], eh64b[:, h * 16:(h + 1) * 16],
                               q64b[:, h * TBW:(h + 1) * TBW],
                               start=(h == 0), stop=(h == 15))
            return ins
        sqm = E("t", sumq_fn, deps=qdone + [sime, r["pc0"]])
        sqe = E("v", (lambda tb: lambda v: v.tensor_copy(
            sumq_sb[:, tb * TBW:(tb + 1) * TBW], pc0[0:16, 0:TBW]))(tb), deps=[sqm])
        r["pc0"] = sqe

        glm = None
        for half in range(2):
            p2done = []
            for hp2 in range(4):
                hp = half * 4 + hp2

                def gk_fn(t, hp=hp):
                    for j2 in range(2):
                        h = hp * 2 + j2
                        ins = t.matmul(pc1[0:64, j2 * TBW:(j2 + 1) * TBW],
                                       GT64[:, :], kn64b[:, h * TBW:(h + 1) * TBW],
                                       start=True, stop=True)
                    return ins
                gkm = E("t", gk_fn, deps=[knb, r["pc1"],
                                          tk_gt if tb == 0 else None,
                                          glm if hp2 == 0 else None])
                gke = E("v", lambda v: v.tensor_copy(gkb[:, :], pc1[0:64, 0:2 * TBW]),
                        deps=[gkm])
                r["pc1"] = gke

                def p2_fn(v, hp=hp, hp2=hp2):
                    for j2 in range(2):
                        hh = hp2 * 2 + j2
                        h = hp * 2 + j2
                        ins = v.tensor_tensor(prodb[:, hh * TBW:(hh + 1) * TBW],
                                              vnb[:, h * TBW:(h + 1) * TBW],
                                              gkb[0:64, j2 * TBW:(j2 + 1) * TBW],
                                              ALU.mult)
                    return ins
                pt = E("v", p2_fn, deps=[gke, vnb_t, simm, sqm])
                p2done.append(pt)

            def gl_fn(t, half=half):
                for hh in range(8):
                    ins = t.matmul(pc0[0:16, 0:TBW],
                                   eh64f[:, (half * 8 + hh) * 16:
                                         (half * 8 + hh + 1) * 16],
                                   prodb[:, hh * TBW:(hh + 1) * TBW],
                                   start=(half == 0 and hh == 0),
                                   stop=(half == 1 and hh == 7))
                return ins
            glm = E("t", gl_fn, deps=p2done + [sqe, r["pc0"] if half == 0 else None])
        gle = E("v", lambda v: v.tensor_copy(glg[:, :], pc0[0:16, 0:TBW]), deps=[glm])
        r["pc0"] = gle
        l1 = E("v", lambda v: v.tensor_scalar(glg[:, :], glg[:, :], kvns[:, 0:1],
                                              wgzb[:, 0:1], ALU.mult, ALU.add),
               deps=[gle, tk_sc, tk_sc2], small=True)
        l2 = E("v", lambda v: v.tensor_scalar(gt1[:, :], glg[:, :], 0.0, None, ALU.max),
               deps=[l1], small=True)
        l3 = E("v", lambda v: v.tensor_scalar(gt2[:, :], glg[:, :], 0.0, None, ALU.min),
               deps=[l2], small=True)
        l4 = E("v", lambda v: v.scalar_tensor_tensor(gt1[:, :], gt2[:, :], 0.01,
                                                     gt1[:, :], ALU.mult, ALU.add),
               deps=[l3], small=True)
        l5 = E("a", lambda a: a.activation(gt2[:, :], gt1[:, :], AF.Square), deps=[l4])
        l6 = E("v", (lambda tb: lambda v: v.tensor_scalar(
            g_sb[:, tb * TBW:(tb + 1) * TBW], gt2[:, :], EPS, None, ALU.add))(tb),
            deps=[l5])
        l7 = E("v", (lambda tb: lambda v: v.tensor_scalar(
            g2_sb[:, tb * TBW:(tb + 1) * TBW], g_sb[:, tb * TBW:(tb + 1) * TBW],
            kvns[:, 0:1], None, ALU.mult))(tb), deps=[l6], small=True)

        st1 = E("s", (lambda tb: lambda s: s.dma_start(
            AP(QFM, tb * TBW, [[16 * 4096, 64], [4096, 16], [1, TBW]]), q64b[:, :]))(tb),
            deps=qdone + [sqm, prod_s], inc=16)
        st2 = E("s", (lambda tb: lambda s: s.dma_start(
            AP(KFM, tb * TBW, [[16 * 4096, 64], [4096, 16], [1, TBW]]), kn64b[:, :]))(tb),
            deps=[knb, prod_s, glm], inc=16)
        st3 = E("s", (lambda tb: lambda s: s.dma_start(
            AP(VFM, tb * TBW, [[16 * 4096, 64], [4096, 16], [1, TBW]]), vnb[:, :]))(tb),
            deps=[vnb_t, glm], inc=16)
        store_deps += [st1, st2, st3]
        r["xtb"] = mmq
        r["pc0"] = l7

    C["attn2a_out"] = dict(sim_sb=sim_sb, g_sb=g_sb, g2_sb=g2_sb, sumq_sb=sumq_sb,
                           Ssb=Ssb, store_deps=store_deps, sc=sc,
                           r=r, pc0=pc0, pc1=pc1, pxt=pxt, pqk=pqk, wobc=wobc)
    return sc


def emit_attn_post(nc, ctx, C, sc):
    NCH = 32
    E = sc.emit
    A = C["attn2a_out"]

    def sb(name, shape, dtype=F32):
        return ctx.enter_context(nc.sbuf_tensor(f"ap_{name}", list(shape), dtype))
    sh = C["shared2"]
    alph, beta, a_c, e2t, e3t = sh["alph"], sh["beta"], sh["a_c"], sh["e2t"], sh["e3t"]
    Vpref, SP, prefw = sh["Vpref"], sh["SP"], sh["prefw"]
    mp_c = sh["mp_c"]
    sim_sb, g_sb, g2_sb, sumq_sb = A["sim_sb"], A["g_sb"], A["g2_sb"], A["sumq_sb"]
    Ssb = A["Ssb"]
    pc0, pc1, pxt, pqk = A["pc0"], A["pc1"], A["pxt"], A["pqk"]
    r = A["r"]
    idn_f = C["idn_f"]
    KFM, VFM = C["KFM"], C["VFM"]
    last2a = A["store_deps"][-1]
    g2T_t = sb2 = None

    M = sb("M", [16, 4096])
    S1 = sb("S1", [16, 4096])
    S2 = sb("S2", [16, 4096])
    S3 = sb("S3", [16, 4096])
    mend = sb("mend", [16, 1])
    ntot = sb("ntot", [16, 1])
    gctot = sb("gctot", [16, 1])
    a2T = sb("a2T", [P, 16], BF16)
    vtl = [sb("vtl0", [P, 1024], BF16)] * 2
    ktl = [sb("ktl0", [P, 1024], BF16)] * 2
    Vrun = sb("Vrun", [1, 1024])
    Sg = sb("Sg", [64, 256])
    Vr1 = sb("Vr1", [1, 512])
    sr8 = sb("sr8", [1, 8 * 48])
    mp_r = sb("mp_r", [1, 16])
    np_r = sb("np_r", [1, 16])
    gp_r = sb("gp_r", [1, 16])
    er_r = sb("er_r", [1, 16])
    t16a = sb("t16a", [1, 16])
    t16b = sb("t16b", [1, 16])
    np_c = sb("np_c", [16, 1])
    gp_c = sb("gp_c", [16, 1])
    ccs_in, ccs_out = C["ccs_in"], C["ccs_out"]

    s1 = E("v", lambda v: v.reduce_max(mend[:, 0:1], sim_sb[:, :], axis=AX.X),
           deps=[last2a])
    s2 = E("v", lambda v: v.tensor_scalar(S1[:, :], sim_sb[:, :], mend[:, 0:1],
                                          None, ALU.subtract), deps=[s1], small=True)
    s3 = E("a", lambda a: a.activation(S1[:, :], S1[:, :], AF.Exp), deps=[s2])
    s4 = E("v", lambda v: v.reduce_sum(ntot[:, 0:1], S1[:, :], axis=AX.X), deps=[s3])
    s5 = E("v", lambda v: v.reduce_sum(gctot[:, 0:1], g_sb[:, :], axis=AX.X),
           deps=[last2a])

    vz = E("v", lambda v: v.memset(Vrun[:, :], 0.0))
    vload_tok = [None, None]
    kload_tok = [None, None]
    prevP = None
    prevS4 = None
    g2T = sb("g2T", [P, 16])
    vg_p = sb("vg_p", [P, 1024], BF16)
    for cg in range(NCH):
        j = cg % 2

        def ldv_fn(s, cg=cg, j=j):
            for h in range(16):
                ins = s.dma_start_transpose(
                    vtl[j][:, h * 64:(h + 1) * 64],
                    AP(VFM, h * 4096 + cg * P, [[16 * 4096, 64], [1, P]]))
                if h < 15:
                    ins.then_inc(sc.sem["s"], 16)
            return ins
        ld = E("s", ldv_fn, deps=[last2a if cg < 2 else None,
                                  vload_tok[0], vload_tok[1]],
               inc=16 * 16, last_inc=16)

        def ldk_fn(s, cg=cg, j=j):
            for h in range(16):
                ins = s.dma_start_transpose(
                    ktl[j][:, h * 64:(h + 1) * 64],
                    AP(KFM, h * 4096 + cg * P, [[16 * 4096, 64], [1, P]]))
                if h < 15:
                    ins.then_inc(sc.sem["s"], 16)
            return ins
        ldk = E("s", ldk_fn, deps=[last2a if cg < 2 else None,
                                   kload_tok[0], kload_tok[1]],
                inc=16 * 16, last_inc=16)
        trm = E("t", (lambda cg: lambda t: t.transpose(
            pc1[:, 0:16], S1[:, cg * P:(cg + 1) * P], idn_f[0:16, 0:16]))(cg),
            deps=[s3, r["pc1"]])
        tre = E("v", lambda v: v.tensor_copy(a2T[:, :], pc1[:, 0:16]), deps=[trm])
        r["pc1"] = tre

        def pfull_fn(t, j=j):
            for h in range(16):
                ins = t.matmul(pxt[0:1, h * 64:(h + 1) * 64],
                               a2T[:, h: h + 1], vtl[j][:, h * 64:(h + 1) * 64],
                               start=True, stop=True)
            return ins
        pm = E("t", pfull_fn, deps=[tre, ld, r["pxt"], prevP])
        va = E("v", lambda v: v.tensor_tensor(Vrun[:, :], Vrun[:, :],
                                              pxt[0:1, 0:1024], ALU.add),
               deps=[pm, vz], small=True)
        r["pxt"] = va
        prevP = va
        g2m = E("t", (lambda cg: lambda t: t.transpose(
            pc1[:, 0:16], g2_sb[:, cg * P:(cg + 1) * P], idn_f[0:16, 0:16]))(cg),
            deps=[last2a, r["pc1"], tre])
        g2e = E("v", lambda v: v.tensor_copy(g2T[:, :], pc1[:, 0:16]), deps=[g2m])
        r["pc1"] = g2e
        vg = E("v", (lambda j: lambda v: v.tensor_tensor(
            vg_p[:, :], vtl[j][:, :],
            bass.AP(g2T, 0, [[16, P], [1, 16], [0, 64]]), ALU.mult))(j),
            deps=[g2e, ld, prevS4])

        def mm4_fn(t, j=j):
            for h in range(16):
                ins = t.matmul(pqk[:, h * 64:(h + 1) * 64],
                               vg_p[:, h * 64:(h + 1) * 64],
                               ktl[j][:, h * 64:(h + 1) * 64],
                               start=(h in (0, 8)), stop=(h in (7, 15)))
            return ins
        m4 = E("t", mm4_fn, deps=[vg, ldk, r["pqk"]])
        prevS4 = E("v", lambda v: v.tensor_tensor(Ssb[:, :], Ssb[:, :],
                                                  pqk[0:64, :], ALU.add),
                   deps=[m4])
        r["pqk"] = prevS4
        vload_tok[j] = vg
        kload_tok[j] = m4

    p1 = E("s", lambda s: s.dma_start(AP(ccs_in, 0, [[1024, 64], [1, 1024]]),
                                      Ssb[:, :]), deps=[prevS4, last2a], inc=16)
    p2 = E("s", lambda s: s.dma_start(AP(ccs_in, 64 * 1024, [[1024, 1], [1, 1024]]),
                                      Vrun[:, :]), deps=[prevP], inc=16)
    p3 = E("s", lambda s: s.dma_start(AP(ccs_in, 65 * 1024, [[1, 16], [1, 1]]),
                                      mend[:, 0:1]), deps=[s1], inc=16)
    p4 = E("s", lambda s: s.dma_start(AP(ccs_in, 65 * 1024 + 16, [[1, 16], [1, 1]]),
                                      ntot[:, 0:1]), deps=[s4], inc=16)
    p5 = E("s", lambda s: s.dma_start(AP(ccs_in, 65 * 1024 + 32, [[1, 16], [1, 1]]),
                                      gctot[:, 0:1]), deps=[s5], inc=16)
    cc2 = E("g", lambda g: g.collective_compute(
        "AllGather", ALU.bypass, replica_groups=[[0, 1, 2, 3, 4, 5, 6, 7]],
        ins=[ccs_in.ap().opt()], outs=[ccs_out.ap().opt()]),
        deps=[p1, p2, p3, p4, p5], inc=1)

    pw = E("s", lambda s: s.dma_start(prefw[:, :], C["prefw"][:, :]), inc=16)

    lr = E("s", lambda s: s.dma_start(
        bass.AP(sr8, 0, [[8 * 48, 1], [48, 8], [1, 48]]),
        AP(ccs_out, 65 * 1024, [[8 * 48, 1], [66 * 1024, 8], [1, 48]]),
    ), deps=[cc2], inc=16)
    sz = E("v", lambda v: v.memset(SP[:, :], 0.0))
    prevS = sz
    for rr in range(8):
        for hf in range(4):
            lS = E("s", (lambda rr, hf: lambda s: s.dma_start(
                Sg[:, :], AP(ccs_out, rr * 66 * 1024 + hf * 256,
                             [[1024, 64], [1, 256]])))(rr, hf),
                deps=[cc2, prevS], inc=16)
            prevS = E("v", (lambda rr, hf: lambda v: v.scalar_tensor_tensor(
                SP[:, hf * 256:(hf + 1) * 256], Sg[:, :], prefw[:, rr: rr + 1],
                SP[:, hf * 256:(hf + 1) * 256],
                ALU.mult, ALU.add))(rr, hf), deps=[lS, pw, prevS])
    mz = E("v", lambda v: v.memset(mp_r[:, :], -1e30))
    prevM = mz
    for rr in range(8):
        t1 = E("v", (lambda rr: lambda v: v.tensor_scalar(
            t16a[:, :], sr8[0:1, rr * 48: rr * 48 + 16],
            prefw[0:1, rr: rr + 1], None, ALU.mult))(rr),
            deps=[lr, pw, prevM], small=True)
        t2 = E("v", (lambda rr: lambda v: v.tensor_scalar(
            t16b[:, :], bass.AP(prefw, rr, [[8, 1], [0, 16]]), 1e30, -1e30,
            ALU.mult, ALU.add))(rr), deps=[pw, t1], small=True)
        t3 = E("v", lambda v: v.tensor_tensor(t16a[:, :], t16a[:, :], t16b[:, :],
                                              ALU.add), deps=[t2], small=True)
        prevM = E("v", lambda v: v.tensor_tensor(mp_r[:, :], mp_r[:, :], t16a[:, :],
                                                 ALU.max), deps=[t3], small=True)
    nz = E("v", lambda v: v.memset(np_r[:, :], 0.0))
    gz = E("v", lambda v: v.memset(gp_r[:, :], 0.0))
    vpz = E("v", lambda v: v.memset(Vpref[:, :], 0.0))
    prevN = [nz, gz, vpz, prevM]
    for rr in range(8):
        d1 = E("v", (lambda rr: lambda v: v.tensor_tensor(
            t16a[:, :], sr8[0:1, rr * 48: rr * 48 + 16], mp_r[:, :], ALU.subtract))(rr),
            deps=prevN, small=True)
        d1b = E("v", (lambda rr: lambda v: v.tensor_scalar(
            t16a[:, :], t16a[:, :], prefw[0:1, rr: rr + 1], None, ALU.mult))(rr),
            deps=[d1, pw], small=True)
        d1c = E("v", (lambda rr: lambda v: v.tensor_scalar(
            t16b[:, :], bass.AP(prefw, rr, [[8, 1], [0, 16]]), 50.0, -50.0,
            ALU.mult, ALU.add))(rr), deps=[pw, d1b], small=True)
        d1d = E("v", lambda v: v.tensor_tensor(t16a[:, :], t16a[:, :], t16b[:, :],
                                               ALU.add), deps=[d1c], small=True)
        d2 = E("a", lambda a: a.activation(er_r[:, :], t16a[:, :], AF.Exp), deps=[d1d])
        d3 = E("v", (lambda rr: lambda v: v.tensor_scalar(
            er_r[:, :], er_r[:, :], prefw[0:1, rr: rr + 1], None, ALU.mult))(rr),
            deps=[d2], small=True)
        d4 = E("v", (lambda rr: lambda v: v.tensor_tensor(
            t16b[:, :], sr8[0:1, rr * 48 + 16: rr * 48 + 32], er_r[:, :],
            ALU.mult))(rr), deps=[d3], small=True)
        d5 = E("v", lambda v: v.tensor_tensor(np_r[:, :], np_r[:, :], t16b[:, :],
                                              ALU.add), deps=[d4], small=True)
        d6 = E("v", (lambda rr: lambda v: v.scalar_tensor_tensor(
            gp_r[:, :], sr8[0:1, rr * 48 + 32: rr * 48 + 48],
            prefw[0:1, rr: rr + 1], gp_r[:, :], ALU.mult, ALU.add))(rr),
            deps=[lr, prevN[1]], small=True)
        d8 = None
        for hf in range(2):
            lv1 = E("s", (lambda rr, hf: lambda s: s.dma_start(
                Vr1[:, :], AP(ccs_out, rr * 66 * 1024 + 64 * 1024 + hf * 512,
                              [[512, 1], [1, 512]])))(rr, hf),
                deps=[cc2, prevN[2], d8], inc=16)
            d7 = E("v", (lambda rr, hf: lambda v: v.tensor_tensor(
                Vrun[0:1, hf * 512:(hf + 1) * 512], Vr1[0:1, :],
                bass.AP(er_r, hf * 8, [[16, 1], [1, 8], [0, 64]]), ALU.mult))(rr, hf),
                deps=[lv1, d3, p2], small=True)
            d8 = E("v", (lambda hf: lambda v: v.tensor_tensor(
                Vpref[0:1, hf * 512:(hf + 1) * 512],
                Vpref[0:1, hf * 512:(hf + 1) * 512],
                Vrun[0:1, hf * 512:(hf + 1) * 512], ALU.add))(hf),
                deps=[d7], small=True)
        prevN = [d5, d6, d8, prevM]
    nfin, gfin, vfin = prevN[0], prevN[1], prevN[2]
    sh["vfin"] = vfin

    cols = []
    for (row, col, dep) in [(mp_r, mp_c, prevM), (np_r, np_c, nfin), (gp_r, gp_c, gfin)]:
        tr = E("t", (lambda row: lambda t: t.transpose(
            pc1[0:16, 0:1], row[:, :], idn_f[0:1, 0:1]))(row), deps=[dep, r["pc1"]])
        ev = E("v", (lambda col: lambda v: v.tensor_copy(col[:, :], pc1[0:16, 0:1]))(col),
               deps=[tr])
        r["pc1"] = ev
        cols.append(ev)
    mcol_tok, ncol_tok, gcol_tok = cols
    sh["mcol_tok"] = mcol_tok

    g1 = E("v", lambda v: v.tensor_tensor_scan(M[:, :], sim_sb[:, :], sim_sb[:, :],
                                               mp_c[:, 0:1], ALU.max, ALU.bypass),
           deps=[mcol_tok, last2a])
    g2_ = E("v", lambda v: v.tensor_copy(S1[:, 1:4096], M[:, 0:4095]),
            deps=[g1, prevP, p2, s4])
    g3 = E("v", lambda v: v.tensor_copy(S1[:, 0:1], mp_c[:, 0:1]),
           deps=[g2_, mcol_tok], small=True)
    g4 = E("v", lambda v: v.tensor_tensor(S1[:, :], S1[:, :], M[:, :],
                                          ALU.subtract), deps=[g3], small=True)
    g5 = E("a", lambda a: a.activation(S1[:, :], S1[:, :], AF.Exp), deps=[g4])
    g6 = E("v", lambda v: v.tensor_tensor(S2[:, :], sim_sb[:, :], M[:, :],
                                          ALU.subtract), deps=[g1])
    g7 = E("a", lambda a: a.activation(S2[:, :], S2[:, :], AF.Exp), deps=[g6])
    g8 = E("v", lambda v: v.tensor_tensor_scan(S3[:, :], S1[:, :], S2[:, :],
                                               np_c[:, 0:1], ALU.mult, ALU.add),
           deps=[g5, g7, ncol_tok])
    h6 = E("v", lambda v: v.tensor_scalar(S3[:, :], S3[:, :], EPS, None, ALU.add),
           deps=[g8], small=True)
    h7 = E("v", lambda v: v.reciprocal(S3[:, :], S3[:, :]), deps=[h6], small=True)
    h8 = E("v", lambda v: v.tensor_tensor(S1[:, :], S2[:, :], S3[:, :], ALU.mult),
           deps=[h7, g8, g5], small=True)
    h9 = E("v", lambda v: v.tensor_tensor(beta[:, :], S1[:, :], sumq_sb[:, :],
                                          ALU.mult), deps=[h8, last2a])
    h10 = E("v", lambda v: v.tensor_tensor(beta[:, :], beta[:, :], S3[:, :],
                                           ALU.mult), deps=[h9], small=True)
    h1 = E("v", lambda v: v.tensor_copy(
        S3[:, :], bass.AP(M, 127, [[4096, 16], [128, 32], [0, 128]])),
        deps=[g1, h10, h8], small=True)
    h2 = E("v", lambda v: v.tensor_tensor(S3[:, :], S3[:, :], M[:, :],
                                          ALU.subtract), deps=[h1], small=True)
    h3 = E("a", lambda a: a.activation(S3[:, :], S3[:, :], AF.Exp), deps=[h2])
    j1 = E("v", lambda v: v.tensor_copy(
        M[:, 0:32], bass.AP(M, 127, [[4096, 16], [128, 32]])), deps=[g1, h2],
        small=True)
    j2 = E("v", lambda v: v.tensor_copy(M[:, 33:64], M[:, 0:31]), deps=[j1],
           small=True)
    j3 = E("v", lambda v: v.tensor_copy(M[:, 32:33], mp_c[:, 0:1]),
           deps=[j2, mcol_tok], small=True)
    j4 = E("v", lambda v: v.tensor_tensor(e2t[:, :], M[:, 32:64], M[:, 0:32],
                                          ALU.subtract), deps=[j3], small=True)
    j5 = E("a", lambda a: a.activation(e2t[:, :], e2t[:, :], AF.Exp), deps=[j4])
    j6 = E("v", lambda v: v.tensor_scalar(e3t[:, :], M[:, 0:32], mp_c[:, 0:1], None,
                                          ALU.subtract), deps=[j1, mcol_tok],
           small=True)
    j7 = E("a", lambda a: a.activation(e3t[:, :], e3t[:, :], AF.Exp, scale=-1.0),
           deps=[j6])
    sh["e2_tok"], sh["e3_tok"] = j5, j7
    h11 = E("v", lambda v: v.tensor_tensor(beta[:, :], beta[:, :], S3[:, :],
                                           ALU.mult), deps=[h10, h3], small=True)
    sh["beta_tok"] = h11
    h4 = E("v", lambda v: v.reciprocal(a_c[:, :], S3[:, :]), deps=[h3, h11])
    h5 = E("v", lambda v: v.tensor_tensor(a_c[:, :], a_c[:, :], S2[:, :], ALU.mult),
           deps=[h4, g7], small=True)
    sh["ac_tok"] = h5
    g9 = E("v", lambda v: v.tensor_tensor_scan(S2[:, :], g_sb[:, :], g_sb[:, :],
                                               gp_c[:, 0:1], ALU.add, ALU.bypass),
           deps=[gcol_tok, last2a, h5, h8])
    h12 = E("v", lambda v: v.tensor_scalar(alph[:, :], S1[:, :], -1.0, 1.0,
                                           ALU.mult, ALU.add), deps=[h8, h9])
    h13 = E("v", lambda v: v.tensor_scalar(S2[:, :], S2[:, :], EPS, None,
                                           ALU.add), deps=[g9], small=True)
    h14 = E("v", lambda v: v.reciprocal(S2[:, :], S2[:, :]), deps=[h13], small=True)
    h15 = E("v", lambda v: v.tensor_tensor(alph[:, :], alph[:, :], S2[:, :],
                                           ALU.mult), deps=[h12, h14], small=True)
    sh["alph_tok"] = h15
    sh["SP_tok"] = prevS
    sh["vt_free"] = vload_tok
    sh["kt_free"] = kload_tok
    sh["post_end"] = [(e, sc.cnt[e], False) for e in sc.ENG]
    if C.get("dbg2") is not None:
        E("s", lambda s: s.dma_start(C["ccs_dbg"][:, :], ccs_out[:, :]),
          deps=[cc2], inc=16)
        dumps = [(sim_sb, last2a), (g_sb, last2a), (sumq_sb, last2a),
                 (M, g1), (S3, h3), (alph, h15), (beta, h11), (a_c, h5),
                 (S1, h8), (S2, h15)]
        pdmp = None
        for i, (tile, dep) in enumerate(dumps):
            pdmp = E("s", (lambda tile, i: lambda s: s.dma_start(
                AP(C["dbg2"], i * 16 * 4096, [[4096, 16], [1, 4096]]),
                tile[:, :]))(tile, i), deps=[dep, pdmp, h15, g9], inc=16)


def emit_attn_loop(nc, ctx, C, sc):
    NCH = 32
    E = sc.emit
    A = C["attn2a_out"]
    sh = C["shared2"]

    def sb(name, shape, dtype=F32):
        return ctx.enter_context(nc.sbuf_tensor(f"al_{name}", list(shape), dtype))
    alph, beta, a_c, e2t, e3t = sh["alph"], sh["beta"], sh["a_c"], sh["e2t"], sh["e3t"]
    Vpref, SP = sh["Vpref"], sh["SP"]
    g2_sb = A["g2_sb"]
    pc0, pc1, pxt, pqk = A["pc0"], A["pc1"], A["pxt"], A["pqk"]
    r = A["r"]
    idn_f, tri_b, ones_c = C["idn_f"], C["tri_b"], C["ones_c"]
    KFM, QFM, VFM = C["KFM"], C["QFM"], C["VFM"]
    X1d, X2d, YNd = C["X1"], C["X2"], C["YN"]

    vtm_l = [sb(f"vtml{j}", [P, 1024], BF16) for j in range(2)]
    ktm_l = [sb(f"ktml{j}", [P, 1024], BF16) for j in range(2)]
    qf_l = [sb(f"qfl{j}", [64, 2048], BF16) for j in range(2)]
    vf_l = [sb(f"vfl{j}", [64, 2048], BF16) for j in range(2)]
    cA = sb("cA", [P, 16])
    cB = sb("cB", [P, 16])
    cC = sb("cC", [P, 16])
    cCb = sb("cCb", [P, 16], BF16)
    cD = sb("cD", [P, 16])
    er2 = sb("er2", [1, 16])
    er3 = sb("er3", [1, 16])
    A_sb = sb("A_sb", [P, 2048], BF16)
    av_t = sb("av_t", [P, 1024], BF16)
    vg_l = sb("vg_l", [P, 1024], BF16)
    Srun = sb("Srun", [64, 1024])
    Srun_b = sb("Srun_b", [64, 1024], BF16)
    Vrun = sb("Vrun", [1, 1024])
    vrow1 = sb("vrow1", [1, 1024])
    vrow2 = sb("vrow2", [1, 1024])
    crow_b = sb("crow_b", [1, 1024], BF16)
    Prow = sb("Prow", [1, 1024])
    ytm = sb("ytm", [P, 1024])
    ytm2 = sb("ytm2", [P, 1024])
    ysq = sb("ysq", [P, 1024])
    yss = sb("yss", [P, 4])
    ysr = sb("ysr", [P, 4])
    yn_b = sb("yn_b", [P, 1024], BF16)

    pe = list(sh["post_end"])
    vz2 = E("v", lambda v: v.memset(Vrun[:, :], 0.0), deps=pe)
    yz = E("v", lambda v: v.memset(yss[:, :], 1.0), deps=pe)
    sc0 = E("v", lambda v: v.tensor_copy(Srun[:, :], SP[:, :]),
            deps=[sh["SP_tok"]] + pe)
    scb = E("v", lambda v: v.tensor_copy(Srun_b[:, :], Srun[:, :]), deps=[sc0],
            small=True)
    ld_tok = {"q": [None, None], "v": [None, None],
              "kt": list(sh["kt_free"]), "vt": list(sh["vt_free"])}
    vcar, scar = vz2, scb
    yst_prev = None
    lastdep = sh["alph_tok"]
    for cg in range(NCH):
        j = cg % 2
        lq = E("s", (lambda cg, j: lambda s: s.dma_start(
            qf_l[j][:, :], AP(QFM, cg * P, [[16 * 4096, 64], [4096, 16], [1, P]])))(cg, j),
            deps=[ld_tok["q"][j]] + ([lastdep] + pe if cg < 2 else []), inc=16)
        lv_ = E("s", (lambda cg, j: lambda s: s.dma_start(
            vf_l[j][:, :], AP(VFM, cg * P, [[16 * 4096, 64], [4096, 16], [1, P]])))(cg, j),
            deps=[ld_tok["v"][j]] + (pe if cg < 2 else []), inc=16)
        def lkt_fn(s, cg=cg, j=j):
            for h in range(16):
                ins = s.dma_start_transpose(
                    ktm_l[j][:, h * 64:(h + 1) * 64],
                    AP(KFM, h * 4096 + cg * P, [[16 * 4096, 64], [1, P]]))
                if h < 15:
                    ins.then_inc(sc.sem["s"], 16)
            return ins
        lkt = E("s", lkt_fn, deps=[ld_tok["kt"][j]] + (pe if cg < 2 else []),
                inc=16 * 16, last_inc=16)

        def lvt_fn(s, cg=cg, j=j):
            for h in range(16):
                ins = s.dma_start_transpose(
                    vtm_l[j][:, h * 64:(h + 1) * 64],
                    AP(VFM, h * 4096 + cg * P, [[16 * 4096, 64], [1, P]]))
                if h < 15:
                    ins.then_inc(sc.sem["s"], 16)
            return ins
        lvt = E("s", lvt_fn, deps=[ld_tok["vt"][j]] + (pe if cg < 2 else []),
                inc=16 * 16, last_inc=16)
        cevs = {}
        for (qty, dst, dep) in [(alph, cA, sh["alph_tok"]), (beta, cB, sh["beta_tok"]),
                                (a_c, cC, sh["ac_tok"]), (g2_sb, cD, None)]:
            tr = E("t", (lambda qty, cg: lambda t: t.transpose(
                pc1[:, 0:16], qty[:, cg * P:(cg + 1) * P], idn_f[0:16, 0:16]))(qty, cg),
                deps=[dep, r["pc1"]] + (pe if cg == 0 else []))
            ev = E("v", (lambda dst: lambda v: v.tensor_copy(dst[:, :], pc1[:, 0:16]))(dst),
                   deps=[tr])
            r["pc1"] = ev
            cevs[id(dst)] = ev
        ccb = E("v", lambda v: v.tensor_copy(cCb[:, :], cC[:, :]),
                deps=[cevs[id(cC)]], small=True)
        tr2 = E("t", (lambda cg: lambda t: t.transpose(
            pc1[0:1, 0:16], e2t[:, cg: cg + 1], idn_f[0:16, 0:16]))(cg),
            deps=[sh["e2_tok"], r["pc1"]])
        ev2 = E("v", lambda v: v.tensor_copy(er2[:, :], pc1[0:1, 0:16]), deps=[tr2])
        tr3 = E("t", (lambda cg: lambda t: t.transpose(
            pc1[0:1, 0:16], e3t[:, cg: cg + 1], idn_f[0:16, 0:16]))(cg),
            deps=[sh["e3_tok"], ev2, r["pc1"]])
        ev3 = E("v", lambda v: v.tensor_copy(er3[:, :], pc1[0:1, 0:16]), deps=[tr3])
        r["pc1"] = ev3
        aev = []
        aslots = [pc0[:, 0:P], pc1[:, 0:P], pxt[:, 0:P], pxt[:, 1024:1024 + P]]
        for h in range(16):
            slot = h % 4

            def mm1_fn(t, h=h, slot=slot, j=j):
                return t.matmul(aslots[slot],
                                vf_l[j][:, h * P:(h + 1) * P],
                                qf_l[j][:, h * P:(h + 1) * P],
                                start=True, stop=True)
            deps_extra = []
            if h < 4:
                deps_extra = [r["pc0"], r["pc1"], r["pxt"]]
            else:
                deps_extra = [aev[h - 4]]
            m1 = E("t", mm1_fn, deps=[lq, lv_] + deps_extra)

            def aev_fn(v, h=h, slot=slot):
                return v.scalar_tensor_tensor(
                    A_sb[:, h * P:(h + 1) * P], aslots[slot],
                    cD[:, h: h + 1], tri_b[:, :], ALU.mult, ALU.mult)
            aev.append(E("v", aev_fn, deps=[m1, cevs[id(cD)]]))
        r["pc0"] = aev[-1]
        r["pc1"] = aev[-1]
        r["pxt"] = aev[-1]

        def mm23_fn(t, j=j):
            for h in range(16):
                t.matmul(pxt[:, h * 64:(h + 1) * 64], A_sb[:, h * P:(h + 1) * P],
                         ktm_l[j][:, h * 64:(h + 1) * 64], start=True, stop=False)
                ins = t.matmul(pxt[:, h * 64:(h + 1) * 64], qf_l[j][:, h * P:(h + 1) * P],
                               Srun_b[:, h * 64:(h + 1) * 64], start=False, stop=True)
            return ins
        m23 = E("t", mm23_fn, deps=aev + [lkt, scar, r["pxt"]])
        av = E("v", (lambda j: lambda v: v.tensor_tensor(
            av_t[:, :], vtm_l[j][:, :],
            bass.AP(cC, 0, [[16, P], [1, 16], [0, 64]]), ALU.mult))(j),
            deps=[lvt, cevs[id(cC)]])

        def mm5_fn(t):
            for half in range(2):
                ins = t.matmul(pxt[:, 1024 + half * 512: 1536 + half * 512],
                               tri_b[:, :], av_t[:, half * 512:(half + 1) * 512],
                               start=True, stop=True)
            return ins
        m5 = E("t", mm5_fn, deps=[av, r["pxt"], m23])
        c1_ = E("v", lambda v: v.tensor_tensor(
            vrow1[:, :], Vrun[:, :], bass.AP(er2, 0, [[16, 1], [1, 16], [0, 64]]),
            ALU.mult), deps=[ev2, vcar], small=True)
        c2_ = E("v", lambda v: v.tensor_tensor(
            vrow2[:, :], Vpref[:, :], bass.AP(er3, 0, [[16, 1], [1, 16], [0, 64]]),
            ALU.mult), deps=[ev3, sh["vfin"]], small=True)
        c3_ = E("v", lambda v: v.tensor_tensor(vrow2[:, :], vrow1[:, :], vrow2[:, :],
                                               ALU.add), deps=[c1_, c2_], small=True)
        c4_ = E("v", lambda v: v.tensor_copy(crow_b[:, :], vrow2[:, :]),
                deps=[c3_], small=True)

        def bc_fn(t):
            for half in range(2):
                ins = t.matmul(pxt[:, 1024 + half * 512: 1536 + half * 512],
                               ones_c[:, :], crow_b[0:1, half * 512:(half + 1) * 512],
                               start=False, stop=True)
            return ins
        bcm = E("t", bc_fn, deps=[c4_, m5])
        pr_toks = []
        for rnd in range(2):
            def pf_fn(t, rnd=rnd, j=j):
                for hh in range(8):
                    h = rnd * 8 + hh
                    ins = t.matmul(pc1[0:1, hh * 64:(hh + 1) * 64],
                                   cCb[:, h: h + 1], vtm_l[j][:, h * 64:(h + 1) * 64],
                                   start=True, stop=True)
                return ins
            pfm = E("t", pf_fn, deps=[ccb, lvt, r["pc1"]])
            pre = E("v", (lambda rnd: lambda v: v.tensor_copy(
                Prow[0:1, rnd * 512:(rnd + 1) * 512], pc1[0:1, 0:512]))(rnd),
                deps=[pfm])
            r["pc1"] = pre
            pr_toks.append(pre)
        c5_ = E("v", lambda v: v.tensor_tensor(Vrun[:, :], vrow1[:, :], Prow[:, :],
                                               ALU.add), deps=pr_toks + [c1_, c3_],
                small=True)
        vcar = c5_
        y1 = E("v", lambda v: v.tensor_tensor(
            ytm[:, :], pxt[:, 0:1024],
            bass.AP(cA, 0, [[16, P], [1, 16], [0, 64]]), ALU.mult),
            deps=[m23, cevs[id(cA)]])
        y2 = E("v", lambda v: v.tensor_tensor(
            ytm2[:, :], pxt[:, 1024:2048],
            bass.AP(cB, 0, [[16, P], [1, 16], [0, 64]]), ALU.mult),
            deps=[bcm, cevs[id(cB)]])
        y3 = E("v", lambda v: v.tensor_tensor(ytm[:, :], ytm[:, :], ytm2[:, :],
                                              ALU.add), deps=[y1, y2])
        r["pxt"] = y3
        vgl = E("v", (lambda j: lambda v: v.tensor_tensor(
            vg_l[:, :], vtm_l[j][:, :],
            bass.AP(cD, 0, [[16, P], [1, 16], [0, 64]]), ALU.mult))(j),
            deps=[lvt, cevs[id(cD)]])

        def m4_fn(t, j=j):
            for h in range(16):
                ins = t.matmul(pqk[:, h * 64:(h + 1) * 64],
                               vg_l[:, h * 64:(h + 1) * 64],
                               ktm_l[j][:, h * 64:(h + 1) * 64],
                               start=(h in (0, 8)), stop=(h in (7, 15)))
            return ins
        m4b = E("t", m4_fn, deps=[vgl, lkt, r["pqk"], m23,
                                  sh["SP_tok"] if cg == 0 else None])
        su = E("v", lambda v: v.tensor_tensor(Srun[:, :], Srun[:, :], pqk[0:64, :],
                                              ALU.add), deps=[m4b, scar])
        scb2 = E("v", lambda v: v.tensor_copy(Srun_b[:, :], Srun[:, :]), deps=[su],
                 small=True)
        scar = scb2
        r["pqk"] = su
        ld_tok["q"][j] = m23
        ld_tok["v"][j] = aev[-1]
        ld_tok["kt"][j] = m4b
        ld_tok["vt"][j] = pr_toks[-1]
        z1 = E("a", lambda a: a.activation(ysq[:, :], ytm[:, :], AF.Square), deps=[y3])
        z2 = E("v", lambda v: v.reduce_sum(yss[:, 0:1], ysq[:, :], axis=AX.X),
               deps=[z1, yz])
        z3 = E("a", lambda a: a.activation(ysr[:, 0:4], yss[:, 0:4], AF.Sqrt), deps=[z2])
        z4 = E("v", lambda v: v.tensor_scalar(ysr[:, 0:4], ysr[:, 0:4], 1e-12, None,
                                              ALU.max), deps=[z3], small=True)
        z5 = E("v", lambda v: v.reciprocal(ysr[:, 0:4], ysr[:, 0:4]), deps=[z4],
               small=True)
        z6 = E("v", lambda v: v.tensor_scalar(yn_b[:, :], ytm[:, :], ysr[:, 0:1],
                                              None, ALU.mult), deps=[z5, yst_prev],
               small=True)
        yst_prev = E("s", (lambda cg: lambda s: s.dma_start(
            AP(YNd, cg * P * 1024, [[1024, P], [1, 1024]]), yn_b[:, :]))(cg),
            deps=[z6], inc=16)

    A["yst_prev"] = yst_prev
    A["m23"] = m23
    A["loop_end"] = [(e, sc.cnt[e], False) for e in sc.ENG]
    return


def emit_attn_2c(nc, ctx, C, sc):
    E = sc.emit
    A = C["attn2a_out"]

    def sb(name, shape, dtype=F32):
        return ctx.enter_context(nc.sbuf_tensor(f"ac_{name}", list(shape), dtype))
    pc0, pc1, pxt = A["pc0"], A["pc1"], A["pxt"]
    r = A["r"]
    idn_f = C["idn_f"]
    X1d, X2d, YNd = C["X1"], C["X2"], C["YN"]
    yst_prev, m23 = A["yst_prev"], A["m23"]
    yfm = sb("yfm", [P, 8 * 512], BF16)
    x1g = sb("x1g", [P, 4096])
    ao_f = sb("ao_f", [P, 4096])
    x2g = sb("x2g", [P, 4096])
    wo_sb = sb("wo_sb", [P, 8192], BF16)

    lwo = E("g", lambda g: g.dma_start(
        wo_sb[:, :], AP(C["woT"], 0, [[1024, P], [P * 1024, 8], [1, 1024]])),
        deps=list(A["loop_end"]), inc=16)
    wobc = A["wobc"]
    prev_x2 = None
    for tb8 in range(8):
        lyfs = []
        for cc in range(8):
            lyf = E("s", (lambda tb8, cc: lambda s: s.dma_start_transpose(
                yfm[:, cc * 512:(cc + 1) * 512],
                AP(YNd, tb8 * 512 * 1024 + cc * P, [[1024, 512], [1, P]])))(tb8, cc),
                deps=([yst_prev] + list(A["loop_end"]) if tb8 == 0 and cc == 0 else
                      [prev_x2 if cc == 0 else None]), inc=16)
            lyfs.append(lyf)
        lx1 = E("s", (lambda tb8: lambda s: s.dma_start(
            x1g[:, :], AP(X1d, tb8 * 512 * 1024,
                          [[1024, P], [P * 1024, 4], [1, 1024]])))(tb8),
            deps=[prev_x2] + (list(A["loop_end"]) if tb8 == 0 else []), inc=16)
        if tb8 == 0 and C.get("wodbg") is not None:
            E("s", lambda s: s.dma_start(C["wodbg"][:, :], wo_sb[:, :]),
              deps=[lwo, lyfs[-1]], inc=16)
            E("s", lambda s: s.dma_start(C["yfdbg"][:, :], yfm[:, :]),
              deps=lyfs, inc=16)
        woe_toks = []
        for half in range(2):
            def wo_fn(t, half=half):
                for d4 in range(4):
                    dd = half * 4 + d4
                    for cc in range(8):
                        ins = t.matmul(pxt[:, d4 * 512:(d4 + 1) * 512],
                                       wo_sb[:, cc * 1024 + dd * P:
                                             cc * 1024 + (dd + 1) * P],
                                       yfm[:, cc * 512:(cc + 1) * 512],
                                       start=(cc == 0), stop=(cc == 7))
                return ins
            wom = E("t", wo_fn, deps=lyfs + [lwo, r["pxt"]])

            def woe_fn(v, half=half):
                for d4 in range(4):
                    dd = half * 4 + d4
                    ins = v.tensor_scalar(ao_f[:, dd * 512:(dd + 1) * 512],
                                          pxt[:, d4 * 512:(d4 + 1) * 512],
                                          wobc[:, dd: dd + 1], None, ALU.add)
                return ins
            woe = E("v", woe_fn, deps=[wom])
            r["pxt"] = woe
            woe_toks.append(woe)
        res_toks = []
        for ti in range(4):
            def trc_fn(t, ti=ti):
                for dd in range(8):
                    pcx = pc0 if dd < 4 else pc1
                    ins = t.transpose(pcx[:, (dd % 4) * P:(dd % 4 + 1) * P],
                                      ao_f[:, dd * 512 + ti * P: dd * 512 + (ti + 1) * P],
                                      idn_f[:, :])
                return ins
            trc = E("t", trc_fn, deps=woe_toks + [r["pc0"], r["pc1"]])

            def res_fn(v, ti=ti):
                for dd in range(8):
                    pcx = pc0 if dd < 4 else pc1
                    ins = v.tensor_tensor(
                        x2g[:, ti * 1024 + dd * P: ti * 1024 + (dd + 1) * P],
                        x1g[:, ti * 1024 + dd * P: ti * 1024 + (dd + 1) * P],
                        pcx[:, (dd % 4) * P:(dd % 4 + 1) * P], ALU.add)
                return ins
            res = E("v", res_fn, deps=[trc, lx1])
            r["pc0"] = res
            r["pc1"] = res
            res_toks.append(res)
        prev_x2 = E("s", (lambda tb8: lambda s: s.dma_start(
            AP(X2d, tb8 * 512 * 1024, [[1024, P], [P * 1024, 4], [1, 1024]]),
            x2g[:, :]))(tb8), deps=res_toks, inc=16)
    return prev_x2


def build(dev=False, limit=None, phases='123'):
    nc = bass.Bass("TRN2", target_bir_lowering=False,
                   dynamic_dma_scratch_size=16384)
    x_in = nc.dram_tensor("x", [T_CORE, D], F32, kind="ExternalInput")
    w1_in = nc.dram_tensor("w1", [E, D, FFH], F32, kind="ExternalInput")
    w3_in = nc.dram_tensor("w3", [E, D, FFH], F32, kind="ExternalInput")
    w2_in = nc.dram_tensor("w2", [E, FFH, D], F32, kind="ExternalInput")
    wg_in = nc.dram_tensor("wgate", [D, E], F32, kind="ExternalInput")
    out = nc.dram_tensor("out", [T_CORE, D], F32, kind="ExternalOutput")

    kindX1 = "ExternalOutput" if dev else "Internal"
    X1 = nc.dram_tensor("X1", [T_CORE, D], F32, kind=kindX1)
    dbg = None
    if dev:
        dbg = {
            "u1": nc.dram_tensor("dbg_u1", [T_CORE, D], BF16, kind="ExternalOutput"),
            "cvt": nc.dram_tensor("dbg_cvt", [NG * P, 32], F32, kind="ExternalOutput"),
            "oa": nc.dram_tensor("dbg_oa", [NG * P, 4096], F32, kind="ExternalOutput"),
        }
    U_ext = nc.dram_tensor("U_ext", [1024 + T_CORE, D], BF16,
                           kind=("ExternalOutput" if dev else "Internal"))

    with ExitStack() as top:
        idn_f = top.enter_context(nc.sbuf_tensor("idn_f", [P, P], F32))
        idn_b = top.enter_context(nc.sbuf_tensor("idn_b", [P, P], BF16))
        tri_b = top.enter_context(nc.sbuf_tensor("tri_b", [P, P], BF16))
        sel_sb = top.enter_context(nc.sbuf_tensor("sel_sb", [8, 8 * P], BF16))
        ones_c = top.enter_context(nc.sbuf_tensor("ones_c", [1, P], BF16))
        wg_sb = top.enter_context(nc.sbuf_tensor("wg_sb", [P, 8 * E], BF16))
        s_pre = top.enter_context(nc.semaphore("s_pre"))

        with nc.Block() as block:
            @block.gpsimd
            def _(g):
                g.memset(idn_f[:, :], 1.0)
                g.affine_select(idn_f[:, :], idn_f[:, :], [[-1, P]], ALU.is_equal,
                                0.0, base=0, channel_multiplier=1)
                g.memset(idn_b[:, :], 1.0)
                g.affine_select(idn_b[:, :], idn_b[:, :], [[-1, P]], ALU.is_equal,
                                0.0, base=0, channel_multiplier=1)
                g.memset(tri_b[:, :], 0.0)
                g.affine_select(tri_b[:, :], tri_b[:, :], [[-1, P]], ALU.is_gt,
                                1.0, base=0, channel_multiplier=1)
                g.memset(ones_c[:, :], 1.0)
                g.memset(sel_sb[:, :], 1.0)
                g.affine_select(sel_sb[:, :], sel_sb[:, :], [[1, 8 * P]], ALU.is_ge,
                                0.0, base=0, channel_multiplier=-P)
                g.affine_select(sel_sb[:, :], sel_sb[:, :], [[-1, 8 * P]], ALU.is_ge,
                                0.0, base=P - 1, channel_multiplier=P)
                g.dma_start(
                    wg_sb[:, :],
                    AP(wg_in, 0, [[E, P], [P * E, 8], [1, E]]),
                ).then_inc(s_pre, 16)
                g.wait_ge(s_pre, 16)

        with ExitStack() as c1:
          if '1' in phases:
            emit_ffn_pass(nc, c1, tag="f1", src=x_in, dst=X1, u_dram=U_ext,
                          w1=w1_in, w3=w3_in, w2=w2_in, wg_sb=wg_sb,
                          sel_sb=sel_sb, idn_f=idn_f, idn_b=idn_b, dbg=dbg,
                          semctx=top)

        C = {
            "idn_f": idn_f, "tri_b": tri_b, "ones_c": ones_c,
            "U_ext": U_ext, "X1": X1,
            "tbank": nc.dram_tensor("tbank", [16 * P, 1920], F32, kind="ExternalInput"),
            "mphiT": nc.dram_tensor("mphiT", [16384, D], F32, kind="ExternalInput"),
            "wqT": nc.dram_tensor("wqT", [D, D], F32, kind="ExternalInput"),
            "wkT": nc.dram_tensor("wkT", [D, D], F32, kind="ExternalInput"),
            "wvT": nc.dram_tensor("wvT", [D, D], F32, kind="ExternalInput"),
            "woT": nc.dram_tensor("woT", [D, D], F32, kind="ExternalInput"),
            "gt": nc.dram_tensor("gt", [64, 64], F32, kind="ExternalInput"),
            "bias_dd": nc.dram_tensor("bias_dd", [2 * 8, P], F32, kind="ExternalInput"),
            "bias_h": nc.dram_tensor("bias_h", [3 * 16, 64], F32, kind="ExternalInput"),
            "kvns": nc.dram_tensor("kvns", [16, 1], F32, kind="ExternalInput"),
            "wgzb": nc.dram_tensor("wgzb", [16, 1], F32, kind="ExternalInput"),
            "halo_mask": nc.dram_tensor("halo_mask", [P, 1], F32, kind="ExternalInput"),
            "halo_off": nc.dram_tensor("halo_off", [1, 1], U32, kind="ExternalInput"),
            "prefw": nc.dram_tensor("prefw", [64, 8], F32, kind="ExternalInput"),
            "QFM": nc.dram_tensor("QFM", [64, 16 * T_CORE], BF16,
                                  kind=("ExternalOutput" if dev else "Internal")),
            "KFM": nc.dram_tensor("KFM", [64, 16 * T_CORE], BF16,
                                  kind=("ExternalOutput" if dev else "Internal")),
            "VFM": nc.dram_tensor("VFM", [64, 16 * T_CORE], BF16,
                                  kind=("ExternalOutput" if dev else "Internal")),
            "YN": nc.dram_tensor("YN", [T_CORE, D], BF16,
                                 kind=("ExternalOutput" if dev else "Internal")),
            "X2": nc.dram_tensor("X2", [T_CORE, D], F32,
                                 kind=("ExternalOutput" if dev else "Internal")),
            "ccu_in": nc.dram_tensor("ccu_in", [1024, D], BF16, kind="Internal"),
            "ccu_out": nc.dram_tensor("ccu_out", [8 * 1024, D], BF16, kind="Internal"),
            "ccs_in": nc.dram_tensor("ccs_in", [66, 1024], F32, kind="Internal"),
            "ccs_out": nc.dram_tensor("ccs_out", [8 * 66, 1024], F32, kind="Internal"),
            "ccs_dbg": (nc.dram_tensor("ccs_dbg", [8 * 66, 1024], F32,
                                       kind="ExternalOutput") if dev else None),
            "dbg2": (nc.dram_tensor("dbg2", [10 * 16, 4096], F32,
                                    kind="ExternalOutput") if dev else None),
            "wodbg": (nc.dram_tensor("wodbg", [P, 8192], BF16,
                                     kind="ExternalOutput") if dev else None),
            "yfdbg": (nc.dram_tensor("yfdbg", [P, 4096], BF16,
                                     kind="ExternalOutput") if dev else None),
            "u_done": None, "x1_done": None,
        }
        sh2 = {}
        with ExitStack() as csh2:
            with ExitStack() as c2a:
                sc = emit_attn(nc, c2a, csh2, top, C)
            def sbsh(name, shape, dtype=F32):
                return csh2.enter_context(nc.sbuf_tensor(f"sh_{name}", list(shape), dtype))
            for nm in ["alph", "beta", "a_c"]:
                sh2[nm] = sbsh(nm, [16, 4096])
            sh2["e2t"] = sbsh("e2t", [16, 32])
            sh2["e3t"] = sbsh("e3t", [16, 32])
            sh2["Vpref"] = sbsh("Vpref", [1, 1024])
            sh2["SP"] = sbsh("SP", [64, 1024])
            sh2["prefw"] = sbsh("prefw", [64, 8])
            sh2["mp_c"] = sbsh("mp_c", [16, 1])
            C["shared2"] = sh2
            with ExitStack() as c2p:
                emit_attn_post(nc, c2p, C, sc)
            with ExitStack() as c2l:
                emit_attn_loop(nc, c2l, C, sc)
            with ExitStack() as c2c:
                emit_attn_2c(nc, c2c, C, sc)
                with nc.Block() as block:
                    sc.run(block, limit=limit)

        with ExitStack() as c3:
            if '3' not in phases:
                s_cp = top.enter_context(nc.semaphore("s_cp"))
                with nc.Block() as block:
                    @block.sync
                    def _(sync):
                        sync.dma_start(out[:, :], X1[:, :]).then_inc(s_cp, 16)
                        sync.wait_ge(s_cp, 16)
                return nc
            emit_ffn_pass(nc, c3, tag="f3", src=C["X2"], dst=out, u_dram=None,
                          w1=w1_in, w3=w3_in, w2=w2_in, wg_sb=wg_sb,
                          sel_sb=sel_sb, idn_f=idn_f, idn_b=idn_b, dbg=None,
                          semctx=top)
    return nc


def _host_prepare(inputs):
    f32 = np.float32
    x = np.ascontiguousarray(inputs["L0_attn_x"], dtype=f32)
    sbas = np.asarray(inputs["L0_attn_spectral_basis"], f32)
    base = np.zeros((16, 2047), f32)
    base[:, 511:1535] = sbas.T
    tbank = np.zeros((16, P, 1920), f32)
    for p in range(P):
        tbank[:, p, :] = base[:, 127 - p: 127 - p + 1920]
    tbank = np.ascontiguousarray(tbank.reshape(16 * P, 1920))
    mphiT = np.ascontiguousarray(np.asarray(inputs["L0_attn_M_phi_w"], f32).T)
    wqT = np.ascontiguousarray(np.asarray(inputs["L0_attn_wq_w"], f32).T)
    wkT = np.ascontiguousarray(np.asarray(inputs["L0_attn_wk_w"], f32).T)
    wvT = np.ascontiguousarray(np.asarray(inputs["L0_attn_wv_w"], f32).T)
    woT = np.ascontiguousarray(np.asarray(inputs["L0_attn_wo_w"], f32).T)
    gt = np.ascontiguousarray(np.asarray(inputs["L0_attn_wg_z_w"], f32)
                              .reshape(64, 64).T)
    bias_dd = np.ascontiguousarray(np.stack([
        np.asarray(inputs["L0_attn_M_phi_b"], f32),
        np.asarray(inputs["L0_attn_wo_b"], f32)]).reshape(2 * 8, P))
    bias_h = np.ascontiguousarray(np.stack([
        np.asarray(inputs["L0_attn_wq_b"], f32),
        np.asarray(inputs["L0_attn_wk_b"], f32),
        np.asarray(inputs["L0_attn_wv_b"], f32)]).reshape(3 * 16, 64))
    kvns = np.ascontiguousarray(np.asarray(inputs["L0_attn_kv_norm_scale"], f32)
                                .reshape(16, 1))
    wgzb = np.ascontiguousarray(
        np.broadcast_to(np.asarray(inputs["L0_attn_wg_z_b"], f32).reshape(1, 1),
                        (16, 1)))
    per_core = []
    for c in range(NCORE):
        b, qd = c // 4, c % 4
        xs = np.ascontiguousarray(x[b, qd * T_CORE:(qd + 1) * T_CORE, :])
        hmask = np.full((P, 1), 0.0 if qd == 0 else 1.0, f32)
        hoff = np.full((1, 1), ((c - 1) % 8) * 1024 * 1024, np.uint32)
        pw = np.zeros((64, 8), f32)
        for rr in range(8):
            if rr // 4 == b and (rr % 4) < qd:
                pw[:, rr] = 1.0
        m = {
            "x": xs,
            "w1": np.ascontiguousarray(inputs["L0_ffn_W1"], f32),
            "w3": np.ascontiguousarray(inputs["L0_ffn_W3"], f32),
            "w2": np.ascontiguousarray(inputs["L0_ffn_W2"], f32),
            "wgate": np.ascontiguousarray(inputs["L0_ffn_Wg"], f32),
            "tbank": tbank, "mphiT": mphiT, "wqT": wqT, "wkT": wkT,
            "wvT": wvT, "woT": woT, "gt": gt, "bias_dd": bias_dd,
            "bias_h": bias_h, "kvns": kvns, "wgzb": wgzb,
            "halo_mask": hmask, "halo_off": hoff, "prefw": pw,
        }
        per_core.append(m)
    return per_core


_LIMIT = None
_PHASES = '123'


def kernel(**inputs):
    per_core = _host_prepare(inputs)
    nc = build(dev=_DEV, limit=_LIMIT, phases=_PHASES)
    res = run_bass_kernel_spmd(nc, per_core, core_ids=list(range(NCORE)))
    outs = [np.asarray(res.results[c]["out"]) for c in range(NCORE)]
    full = np.empty((2, 16384, D), dtype=np.float32)
    for c in range(NCORE):
        b, qd = c // 4, c % 4
        full[b, qd * T_CORE:(qd + 1) * T_CORE, :] = outs[c]
    if _DEV:
        kernel._dev_results = res
    return full
